# version 01 (final)
# baseline (speedup 1.0000x reference)
import numpy as np
import ml_dtypes
import concourse.bass as bass
import concourse.mybir as mybir
from concourse.bass_utils import run_bass_kernel_spmd

F32 = mybir.dt.float32
BF16 = mybir.dt.bfloat16
AF = mybir.ActivationFunctionType
ALU = mybir.AluOpType
AX = mybir.AxisListType
BFDT = ml_dtypes.bfloat16

NCORES = 8
D = 1024
C = 8
H = 16
HD = 64
WIN = 256
DH = 4096
HC = 32
BATCH = 16
SEQ = 4096
TOK = 2 * SEQ
T = 512
NT = TOK // T
NW = T // WIN
SCALE = HD ** -0.5
EPS = 1e-5
P = 128
SLOT = [0, 2, 1, 3]

ENGS = ("pe", "act", "dve", "pool", "sp")
ENGMAP = {"pe": "tensor", "act": "scalar", "dve": "vector", "pool": "gpsimd", "sp": "sync"}


class Sched:
    def __init__(self):
        self.items = {e: [] for e in ENGS}
        self.emitted = {e: 0 for e in ENGS}
        self.counts = {}
        self.sems = {}
        self.watermark = {e: {} for e in ENGS}
        self.barrier = {}
        self.tracelog = []

    def emit(self, eng, fn, deps, dma, tag=None):
        idx = len(self.items[eng])
        self.items[eng].append({"fn": fn, "deps": deps, "dma": dma,
                                "inc": False, "val": None, "tag": tag})
        return (eng, idx)

    def resolve(self, tok):
        (de, di) = tok
        dit = self.items[de][di]
        if di >= self.emitted[de] or dit["inc"]:
            return tok
        key = (de, "d" if dit["dma"] else "x")
        rep = self.barrier.get(key)
        assert rep is not None, f"stale dep {de}/{di} with no barrier"
        return rep

    def flush(self, nc, block):
        prev_barrier = dict(self.barrier)
        for e in ENGS:
            last = {"d": None, "x": None}
            for i in range(self.emitted[e], len(self.items[e])):
                it = self.items[e][i]
                if it["dma"]:
                    it["inc"] = True
                last["d" if it["dma"] else "x"] = i
            for kind, i in last.items():
                if i is not None:
                    self.items[e][i]["inc"] = True
                    self.barrier[(e, kind)] = (e, i)
        for e in ENGS:
            for it in self.items[e][self.emitted[e]:]:
                for (de, di) in it["deps"]:
                    dit = self.items[de][di]
                    if di < self.emitted[de]:
                        assert dit["inc"], f"cross-round dep on un-inc'd {de}/{di}"
                        continue
                    if de != e or dit["dma"]:
                        dit["inc"] = True
        for e in ENGS:
            cd = self.counts.get((e, "d"), 0)
            cx = self.counts.get((e, "x"), 0)
            for it in self.items[e][self.emitted[e]:]:
                if it["inc"]:
                    if it["dma"]:
                        cd += 16
                        it["val"] = ("d", cd)
                    else:
                        cx += 1
                        it["val"] = ("x", cx)
            self.counts[(e, "d")] = cd
            self.counts[(e, "x")] = cx
        for e in ENGS:
            todo = self.items[e][self.emitted[e]:]
            if not todo:
                continue
            self.emitted[e] = len(self.items[e])

            def body(engobj, e=e, todo=todo, prev_barrier=prev_barrier):
                wm = self.watermark[e]
                for (be, kind), (de, di) in prev_barrier.items():
                    dit = self.items[de][di]
                    if de == e and not dit["dma"]:
                        continue
                    _, val = dit["val"]
                    key = (de, kind)
                    if wm.get(key, 0) < val:
                        engobj.wait_ge(self.sems[key], val)
                        wm[key] = val
                for it in todo:
                    tr = [] if it["tag"] else None
                    for (de, di) in it["deps"]:
                        dit = self.items[de][di]
                        if de == e and not dit["dma"]:
                            if tr is not None:
                                tr.append((de, di, dit["tag"], "fifo"))
                            continue
                        kind, val = dit["val"]
                        key = (de, kind)
                        if wm.get(key, 0) >= val:
                            if tr is not None:
                                tr.append((de, di, dit["tag"], f"dedup{val}"))
                            continue
                        engobj.wait_ge(self.sems[key], val)
                        wm[key] = val
                        if tr is not None:
                            tr.append((de, di, dit["tag"], f"wait{val}"))
                    if tr is not None:
                        self.tracelog.append((e, it["tag"], it["val"], tr))
                    inst = it["fn"](engobj)
                    if it["inc"]:
                        kind, _ = it["val"]
                        inst.then_inc(self.sems[(e, kind)], 16 if it["dma"] else 1)

            getattr(block, ENGMAP[e])(body)


class Buf:
    __slots__ = ("w", "r")

    def __init__(self):
        self.w = []
        self.r = []


class K:
    def __init__(self):
        self.s = Sched()

    def op(self, eng, fn, reads=(), writes=(), deps=(), dma=False, tag=None):
        dl = list(deps)
        for b in reads:
            dl += b.w
        for b in writes:
            dl += b.r + b.w
        seen, dl2 = set(), []
        for t in dl:
            if t is None:
                continue
            t = self.s.resolve(t)
            if t not in seen:
                seen.add(t)
                dl2.append(t)
        tok = self.s.emit(eng, fn, dl2, dma, tag)
        for b in reads:
            b.r.append(tok)
        for b in writes:
            b.w = [tok]
            b.r = []
        return tok


def _pk(i, sigma):
    return (i // (2 * sigma)) * sigma + (i % sigma)


def stage_lhsT_inchunk(w, order_transposed):
    out = np.zeros((7, HC, P, P), np.float32)
    for k in range(7):
        s = (6 - k) if order_transposed else k
        sigma = 1 << s
        for c in range(HC):
            blk = out[k, c]
            base = c * P
            for la in range(P):
                i = base + la
                if (i // sigma) % 2 != 0:
                    continue
                lb = la + sigma
                pk = _pk(i, sigma)
                w00, w01 = w[s, pk, 0, 0], w[s, pk, 0, 1]
                w10, w11 = w[s, pk, 1, 0], w[s, pk, 1, 1]
                if order_transposed:
                    w01, w10 = w10, w01
                blk[la, la] = w00
                blk[lb, la] = w01
                blk[la, lb] = w10
                blk[lb, lb] = w11
    return out


def diag_pairs(stage):
    cs = (1 << stage) // P
    return [(ca, ca + cs) for ca in range(HC) if (ca // cs) % 2 == 0]


def stage_diag_blocks(w, order_transposed):
    out = np.zeros((5, HC, 2, P, P), np.float32)
    for k in range(5):
        s = (11 - k) if order_transposed else (7 + k)
        for (ca, cb) in diag_pairs(s):
            for p in range(P):
                i = ca * P + p
                pk = _pk(i, 1 << s)
                w00, w01 = w[s, pk, 0, 0], w[s, pk, 0, 1]
                w10, w11 = w[s, pk, 1, 0], w[s, pk, 1, 1]
                if order_transposed:
                    w01, w10 = w10, w01
                out[k, ca, 0, p, p] = w00
                out[k, ca, 1, p, p] = w01
                out[k, cb, 0, p, p] = w10
                out[k, cb, 1, p, p] = w11
    return out


def diag_srcs(k, order_transposed):
    s = (11 - k) if order_transposed else (7 + k)
    srcs = {}
    for (ca, cb) in diag_pairs(s):
        srcs[ca] = (ca, cb)
        srcs[cb] = (ca, cb)
    return srcs


def chunked(a):
    d0 = a.shape[0]
    r = a.reshape(d0 // P, P, *a.shape[1:])
    return np.ascontiguousarray(np.moveaxis(r, 1, 0))


def build_program(dbg=None):
    nc = bass.Bass("TRN2", target_bir_lowering=False, debug=False)
    di = lambda n, s, d: nc.dram_tensor(n, s, d, kind="ExternalInput").ap()

    x_in = di("x_in", [P, C, TOK], F32)
    qw_in = di("qw_in", [P, C, D], F32)
    kvw_in = di("kvw_in", [P, C, 2 * HD], F32)
    pw_in = di("pw_in", [P, C, D], F32)
    pb_in = di("pb_in", [P, C], F32)
    wexpT_in = di("wexpT_in", [P, HC, D], F32)
    wconN_in = di("wconN_in", [P, HC, D], F32)
    bexp_in = di("bexp_in", [P, HC, 2], F32)
    upb_in = di("upb_in", [P, HC], F32)
    bcon_in = di("bcon_in", [P, C], F32)
    dwnb_in = di("dwnb_in", [P, C], F32)
    icU_in = di("icU_in", [7, HC, P, P], F32)
    icD_in = di("icD_in", [7, HC, P, P], F32)
    dgU_in = di("dgU_in", [5, HC, 2, P, P], F32)
    dgD_in = di("dgD_in", [5, HC, 2, P, P], F32)
    ident_in = di("ident_in", [P, P], BF16)
    onem_in = di("onem_in", [P, P], BF16)
    onecol_in = di("onecol_in", [P, 1], BF16)
    onerow_in = di("onerow_in", [1, P], BF16)
    c1f_in = di("c1f_in", [1, 1], F32)

    out_t = nc.dram_tensor("out_t", [P, C, TOK], F32, kind="ExternalOutput").ap()
    wexpP = nc.dram_tensor("wexpP", [16, C, P, 256], BF16, kind="Internal").ap()
    wconP = nc.dram_tensor("wconP", [C, HC, P, P], BF16, kind="Internal").ap()
    if dbg == "comp":
        dbgE = nc.dram_tensor("dbgE", [16, C, P, 256], BF16, kind="ExternalOutput").ap()
        dbgC = nc.dram_tensor("dbgC", [C, HC, P, P], BF16, kind="ExternalOutput").ap()

    from contextlib import ExitStack
    with ExitStack() as ctx:
        e = ctx.enter_context
        KK = K()
        S = KK.s
        for en in ENGS:
            for kind in ("x", "d"):
                S.sems[(en, kind)] = e(nc.semaphore(f"s_{en}_{kind}"))
        block = e(nc.Block())

        QW = e(nc.sbuf_tensor("QW", [P, C, D], BF16))
        KVW = e(nc.sbuf_tensor("KVW", [P, C, 2 * HD], BF16))
        PW = e(nc.sbuf_tensor("PW", [P, C, D], BF16))
        IDENT = e(nc.sbuf_tensor("IDENT", [P, P], BF16))
        ONEM = e(nc.sbuf_tensor("ONEM", [P, P], BF16))
        ONECOL = e(nc.sbuf_tensor("ONECOL", [P, 1], BF16))
        ONEROW = e(nc.sbuf_tensor("ONEROW", [1, P], BF16))
        C1F = e(nc.sbuf_tensor("C1F", [1, 1], F32))
        PBF = e(nc.sbuf_tensor("PBF", [P, C], F32))
        PBBF = e(nc.sbuf_tensor("PBBF", [P, C], BF16))
        PBC = e(nc.sbuf_tensor("PBC", [P, C], F32))
        PBMC = e(nc.sbuf_tensor("PBMC", [P, 1], F32))
        UPB = e(nc.sbuf_tensor("UPB", [P, HC], F32))
        B1COL = e(nc.sbuf_tensor("B1COL", [P, HC], F32))
        B2COL = e(nc.sbuf_tensor("B2COL", [P, C], F32))
        BCON = e(nc.sbuf_tensor("BCON", [P, C], F32))
        DWNB = e(nc.sbuf_tensor("DWNB", [P, C], F32))
        QWSUM = e(nc.sbuf_tensor("QWSUM", [1, D], BF16))
        KVWSUM = e(nc.sbuf_tensor("KVWSUM", [1, 2 * HD], BF16))
        PWRS = e(nc.sbuf_tensor("PWRS", [P, C], F32))
        EPSC = e(nc.sbuf_tensor("EPSC", [P, 1], F32))
        PWRSB = e(nc.sbuf_tensor("PWRSB", [P, C], BF16))

        PS = e(nc.psum_tensor("PS", [P, 8, 512], F32))
        PSB = PS.bitcast(BF16)

        def bank(b, n0=0, n1=512):
            return PS[:, b, n0:n1]

        def sview(pr):
            return PS[:, 2 * pr:2 * pr + 2, :].rearrange("p b (s n) -> p (b s) n", n=256)

        def ptview():
            return PS[:, 6:8, :].rearrange("p u (s n) -> p u s n", n=128)

        bPS = [Buf() for _ in range(8)]
        bk = lambda *bs: [bPS[b] for b in bs]

        bQW, bKVW, bPW, bCONST = Buf(), Buf(), Buf(), Buf()
        bPBF, bPBBF, bPBC, bPBMC = Buf(), Buf(), Buf(), Buf()
        bUPB, bB1, bB2, bBCON, bDWNB = Buf(), Buf(), Buf(), Buf(), Buf()
        bQWS, bKVWS, bPWRS = Buf(), Buf(), Buf()
        bWEXP_H, bWCON_H = Buf(), Buf()

        bEPS = Buf()

        def ld(eng, dst, src, wb, deps=(), reads=()):
            return KK.op(eng, lambda g, d=dst, s=src: g.dma_start(out=d, in_=s),
                         reads=list(reads), writes=list(wb), deps=deps, dma=True)

        KK.op("dve", lambda v: v.memset(EPSC[:], EPS), writes=[bEPS])
        ld("sp", IDENT[:], ident_in, [bCONST])
        ld("sp", ONEM[:], onem_in, [bCONST])
        ld("sp", ONECOL[:], onecol_in, [bCONST])
        ld("sp", ONEROW[:], onerow_in, [bCONST])
        ld("sp", C1F[:], c1f_in, [bCONST])
        ld("sp", PBF[:], pb_in, [bPBF])
        ld("sp", UPB[:], upb_in, [bUPB])
        ld("sp", BCON[:], bcon_in, [bBCON])
        ld("sp", DWNB[:], dwnb_in, [bDWNB])
        ld("pool", QW[:], qw_in, [bQW])
        ld("pool", KVW[:], kvw_in, [bKVW])
        ld("pool", PW[:], pw_in, [bPW])

        def colsum(dstrow, W, bW, width, bDst):
            for h2 in range((width + 511) // 512):
                n0, n1 = h2 * 512, min(width, h2 * 512 + 512)
                for kc in range(C):
                    KK.op("pe", lambda t, kc=kc, n0=n0, n1=n1: t.matmul(
                        bank(7, 0, n1 - n0)[0:1, :], ONECOL[:], W[:, kc, n0:n1],
                        start=(kc == 0), stop=(kc == C - 1)),
                        reads=[bW, bCONST], writes=bk(7))
                KK.op("act", lambda s, n0=n0, n1=n1: s.activation(
                    out=dstrow[0:1, n0:n1], in_=bank(7, 0, n1 - n0)[0:1, :],
                    func=AF.Copy), reads=bk(7), writes=[bDst])

        colsum(QWSUM, QW, bQW, D, bQWS)
        colsum(KVWSUM, KVW, bKVW, 2 * HD, bKVWS)
        KK.op("dve", lambda v: v.reduce_sum(out=PWRS[:], in_=PW[:], axis=AX.X),
              reads=[bPW], writes=[bPWRS])
        KK.op("dve", lambda v: v.tensor_copy(out=PWRSB[:], in_=PWRS[:]),
              reads=[bPWRS], writes=[bPWRS])
        KK.op("dve", lambda v: v.tensor_copy(out=PBBF[:], in_=PBF[:]),
              reads=[bPBF], writes=[bPBBF])
        KK.op("pe", lambda t: t.matmul(bank(7, 0, C), ONEM[:], PBBF[:],
                                       start=True, stop=True),
              reads=[bPBBF, bCONST], writes=bk(7))
        KK.op("dve", lambda v: v.reduce_sum(out=PBMC[:], in_=bank(7, 0, C), axis=AX.X),
              reads=bk(7), writes=[bPBMC])
        KK.op("dve", lambda v: v.tensor_scalar(out=PBC[:], in0=PBF[:], scalar1=PBMC[:],
                                               scalar2=None, op0=ALU.subtract),
              reads=[bPBF, bPBMC], writes=[bPBC])
        KK.op("dve", lambda v: v.tensor_tensor(out=B2COL[:], in0=BCON[:], in1=DWNB[:],
                                               op=ALU.add),
              reads=[bBCON, bDWNB], writes=[bB2])

        with ExitStack() as cctx:
            ce = cctx.enter_context
            WCB = ce(nc.sbuf_tensor("WCB", [P, HC, D], BF16))
            TCB = ce(nc.sbuf_tensor("TCB", [P, HC, P], BF16))
            STG = ce(nc.sbuf_tensor("STG", [P, 2, HC, P], BF16))
            DGS = ce(nc.sbuf_tensor("DGS", [P, 2, HC, 2, P], BF16))
            OSTG = ce(nc.sbuf_tensor("OSTG", [P, HC, P], BF16))
            BXB = ce(nc.sbuf_tensor("BXB", [P, HC, 2], BF16))

            bWC = [Buf() for _ in range(HC)]
            bTC = [Buf() for _ in range(HC)]
            bSTG = [Buf(), Buf()]
            bDGS = [Buf(), Buf()]
            bOSTG = Buf()
            bBX = [Buf() for _ in range(HC)]

            def compose_tc(ic_in):
                for s7 in range(7):
                    sl = s7 % 2
                    ld("pool", STG[:, sl], ic_in[s7].rearrange("c r m -> r c m"), [bSTG[sl]])
                    for c in range(HC):
                        b = c % 4
                        rhs = IDENT[:] if s7 == 0 else TCB[:, c, :]
                        rr = [bSTG[sl], bCONST] if s7 == 0 else [bSTG[sl], bTC[c]]
                        KK.op("pe", lambda t, c=c, sl=sl, b=b, rhs=rhs: t.matmul(
                            bank(b, 0, P), STG[:, sl, c, :], rhs, start=True, stop=True),
                            reads=rr, writes=bk(b))
                        eng = "act" if c % 2 else "dve"
                        if eng == "act":
                            KK.op("act", lambda s_, c=c, b=b: s_.activation(
                                out=TCB[:, c, :], in_=bank(b, 0, P), func=AF.Identity),
                                reads=bk(b), writes=[bTC[c]])
                        else:
                            KK.op("dve", lambda v, c=c, b=b: v.tensor_copy(
                                out=TCB[:, c, :], in_=bank(b, 0, P)),
                                reads=bk(b), writes=[bTC[c]])
                for c in range(HC):
                    b = c % 4
                    KK.op("pe", lambda t, c=c, b=b: t.transpose(
                        PSB[:, b, 0:P], TCB[:, c, :], IDENT[:]),
                        reads=[bTC[c], bCONST], writes=bk(b))
                    eng = "act" if c % 2 else "dve"
                    if eng == "act":
                        KK.op("act", lambda s_, c=c, b=b: s_.activation(
                            out=TCB[:, c, :], in_=PSB[:, b, 0:P], func=AF.Identity),
                            reads=bk(b), writes=[bTC[c]])
                    else:
                        KK.op("dve", lambda v, c=c, b=b: v.tensor_copy(
                            out=TCB[:, c, :], in_=PSB[:, b, 0:P]),
                            reads=bk(b), writes=[bTC[c]])

            def apply_tc(buf, bufs, width):
                nh = (width + 511) // 512
                for c in range(HC):
                    for h2 in range(nh):
                        n0, n1 = h2 * 512, min(width, h2 * 512 + 512)
                        b = (2 * c + h2) % 4
                        KK.op("pe", lambda t, c=c, n0=n0, n1=n1, b=b, buf=buf: t.matmul(
                            bank(b, 0, n1 - n0), TCB[:, c, :], buf[:, c, n0:n1],
                            start=True, stop=True),
                            reads=[bTC[c], bufs[c]], writes=bk(b))
                        eng = "act" if (c + h2) % 2 else "dve"
                        if eng == "act":
                            KK.op("act", lambda s_, c=c, n0=n0, n1=n1, b=b, buf=buf:
                                  s_.activation(out=buf[:, c, n0:n1],
                                                in_=bank(b, 0, n1 - n0), func=AF.Identity),
                                  reads=bk(b), writes=[bufs[c]])
                        else:
                            KK.op("dve", lambda v, c=c, n0=n0, n1=n1, b=b, buf=buf:
                                  v.tensor_copy(out=buf[:, c, n0:n1],
                                                in_=bank(b, 0, n1 - n0)),
                                  reads=bk(b), writes=[bufs[c]])

            def diag_stages(dg_in, transposed, buf, bufs, width):
                nh = (width + 511) // 512
                for k5 in range(5):
                    sl = k5 % 2
                    ld("pool", DGS[:, sl], dg_in[k5].rearrange("c j r m -> r c j m"),
                       [bDGS[sl]])
                    srcs = diag_srcs(k5, transposed)
                    for (ca, cb) in diag_pairs((11 - k5) if transposed else (7 + k5)):
                        for h2 in range(nh):
                            n0, n1 = h2 * 512, min(width, h2 * 512 + 512)
                            ba, bb = (2 * h2) % 4, (2 * h2 + 1) % 4
                            for oc, b in ((ca, ba), (cb, bb)):
                                s0, s1 = srcs[oc]
                                KK.op("pe", lambda t, oc=oc, s0=s0, sl=sl, n0=n0, n1=n1, b=b, buf=buf:
                                      t.matmul(bank(b, 0, n1 - n0), DGS[:, sl, oc, 0, :],
                                               buf[:, s0, n0:n1], start=True, stop=False),
                                      reads=[bDGS[sl], bufs[s0]], writes=bk(b))
                                KK.op("pe", lambda t, oc=oc, s1=s1, sl=sl, n0=n0, n1=n1, b=b, buf=buf:
                                      t.matmul(bank(b, 0, n1 - n0), DGS[:, sl, oc, 1, :],
                                               buf[:, s1, n0:n1], start=False, stop=True),
                                      reads=[bDGS[sl], bufs[s1]], writes=bk(b))
                            for oc, b in ((ca, ba), (cb, bb)):
                                eng = "act" if oc % 2 else "dve"
                                if eng == "act":
                                    KK.op("act", lambda s_, oc=oc, n0=n0, n1=n1, b=b, buf=buf:
                                          s_.activation(out=buf[:, oc, n0:n1],
                                                        in_=bank(b, 0, n1 - n0),
                                                        func=AF.Identity),
                                          reads=bk(b), writes=[bufs[oc]])
                                else:
                                    KK.op("dve", lambda v, oc=oc, n0=n0, n1=n1, b=b, buf=buf:
                                          v.tensor_copy(out=buf[:, oc, n0:n1],
                                                        in_=bank(b, 0, n1 - n0)),
                                          reads=bk(b), writes=[bufs[oc]])

            compose_tc(icU_in)
            ld("pool", BXB[:], bexp_in, [bBX[c] for c in range(HC)])
            apply_tc(BXB, bBX, 2)
            diag_stages(dgU_in, False, BXB, bBX, 2)
            KK.op("dve", lambda v: v.tensor_tensor(
                out=B1COL[:], in0=BXB[:, :, 0], in1=UPB[:], op=ALU.add),
                reads=[bUPB] + bBX, writes=[bB1])

            ld("pool", WCB[:], wexpT_in, bWC)
            apply_tc(WCB, bWC, D)
            diag_stages(dgU_in, False, WCB, bWC, D)
            for kc in range(C):
                for hc in range(HC):
                    b = hc % 4
                    KK.op("pe", lambda t, kc=kc, hc=hc, b=b: t.transpose(
                        PSB[:, b, 0:P], WCB[:, hc, kc * P:(kc + 1) * P], IDENT[:]),
                        reads=[bWC[hc], bCONST], writes=bk(b))
                    eng = "act" if hc % 2 else "dve"
                    if eng == "act":
                        KK.op("act", lambda s_, hc=hc, b=b: s_.activation(
                            out=OSTG[:, hc, :], in_=PSB[:, b, 0:P], func=AF.Identity),
                            reads=bk(b), writes=[bOSTG])
                    else:
                        KK.op("dve", lambda v, hc=hc, b=b: v.tensor_copy(
                            out=OSTG[:, hc, :], in_=PSB[:, b, 0:P]),
                            reads=bk(b), writes=[bOSTG])
                for h2 in range(2):
                    dv = wexpP[:, kc, :, h2 * P:(h2 + 1) * P].rearrange(
                        "ms p i -> p ms i")
                    sv_ = OSTG[:].rearrange("p (ms h2) i -> p ms h2 i", h2=2)[:, :, h2, :]
                    KK.op("sp", lambda g, dv=dv, sv_=sv_: g.dma_start(out=dv, in_=sv_),
                          reads=[bOSTG], writes=[bWEXP_H], dma=True)

            compose_tc(icD_in)
            ld("pool", WCB[:], wconN_in, bWC)
            diag_stages(dgD_in, True, WCB, bWC, D)
            apply_tc(WCB, bWC, D)
            for mc in range(C):
                dv = wconP[mc].rearrange("kc p m -> p kc m")
                sv_ = WCB[:, :, mc * P:(mc + 1) * P]
                KK.op("sp", lambda g, dv=dv, sv_=sv_: g.dma_start(out=dv, in_=sv_),
                      reads=bWC, writes=[bWCON_H], dma=True)

            dbg_toks = []
            if dbg == "comp":
                dbg_toks.append(KK.op("sp", lambda g: g.dma_start(
                    out=dbgE.flatten(), in_=wexpP.flatten()),
                    reads=[bWEXP_H], dma=True))
                dbg_toks.append(KK.op("sp", lambda g: g.dma_start(
                    out=dbgC.flatten(), in_=wconP.flatten()),
                    reads=[bWCON_H], dma=True))

            S.flush(nc, block)

        XBUF = e(nc.sbuf_tensor("XBUF", [P, 2, C, T], F32))
        XBF = e(nc.sbuf_tensor("XBF", [P, C, T], BF16))
        LNBF = e(nc.sbuf_tensor("LNBF", [P, C, T], BF16))
        QBUF = e(nc.sbuf_tensor("QBUF", [P, C, T], BF16))
        KVS = e(nc.sbuf_tensor("KVS", [P, T], BF16))
        K2S = e(nc.sbuf_tensor("K2S", [P, T], BF16))
        VT = e(nc.sbuf_tensor("VT", [P, 2, 2, HD], BF16))
        EB2 = e(nc.sbuf_tensor("EB2", [P, 2, 4, WIN], BF16))
        PB = e(nc.sbuf_tensor("PB", [P, 4, WIN], BF16))
        PTS = e(nc.sbuf_tensor("PTS", [P, 2, 2, 4, P], BF16))
        AOB = e(nc.sbuf_tensor("AOB", [P, C, WIN], BF16))
        YB = e(nc.sbuf_tensor("YB", [P, C, T], F32))
        GB = e(nc.sbuf_tensor("GB", [P, HC, T], BF16))
        SQ = e(nc.sbuf_tensor("SQ", [P, C, T], BF16))
        WEXPS = e(nc.sbuf_tensor("WEXPS", [P, 2, C, 256], BF16))
        WCONS = e(nc.sbuf_tensor("WCONS", [P, 2, HC, P], BF16))
        MACC = e(nc.sbuf_tensor("MACC", [P, T], F32))
        MACCB = e(nc.sbuf_tensor("MACCB", [P, T], BF16))
        M1S = e(nc.sbuf_tensor("M1S", [P, T], F32))
        M2S = e(nc.sbuf_tensor("M2S", [P, T], F32))
        RA = e(nc.sbuf_tensor("RA", [P, T], F32))
        RB = e(nc.sbuf_tensor("RB", [P, T], F32))
        NEGM1 = e(nc.sbuf_tensor("NEGM1", [1, T], BF16))
        NEGM2 = e(nc.sbuf_tensor("NEGM2", [1, T], BF16))
        R1C = e(nc.sbuf_tensor("R1C", [P, 4], F32))
        NMX = e(nc.sbuf_tensor("NMX", [P, 4], F32))
        BS4 = e(nc.sbuf_tensor("BS4", [P, 4], F32))
        SUM4 = e(nc.sbuf_tensor("SUM4", [P, 4], F32))
        REC4 = e(nc.sbuf_tensor("REC4", [P, 4], F32))

        bXB = [Buf(), Buf()]
        bXBF, bLNBF, bQBUF = Buf(), Buf(), Buf()
        bKVS, bK2, bVT, bPTS = Buf(), Buf(), Buf(), Buf()
        bEB2 = [Buf(), Buf()]
        bPB = Buf()
        bAOB, bYB, bSQ = Buf(), Buf(), Buf()
        bGB = [Buf() for _ in range(HC)]
        bWES = [Buf(), Buf()]
        bWCS = [Buf(), Buf()]
        bMACC, bMACCB, bM1S, bRA, bRB = Buf(), Buf(), Buf(), Buf(), Buf()
        bM2S = Buf()
        bNM1, bNM2 = Buf(), Buf()
        bR1C, bNMX, bBS4, bSUM4, bREC4 = Buf(), Buf(), Buf(), Buf(), Buf()

        out_toks = []

        def dump_f32(KK, src_view, bsrc, region_k, XB1, bXB1, nchunk=C):
            KK.op("dve", lambda v, src_view=src_view, XB1=XB1, nchunk=nchunk:
                  v.tensor_copy(out=XB1[:, 0:nchunk, :], in_=src_view),
                  reads=[bsrc], writes=[bXB1])
            out_toks.append(KK.op(
                "pool", lambda g_, region_k=region_k, XB1=XB1, nchunk=nchunk: g_.dma_start(
                    out=out_t[:, 0:nchunk, region_k * T:(region_k + 1) * T],
                    in_=XB1[:, 0:nchunk, :]),
                reads=[bXB1], dma=True))

        def accum_stats(V, nchunk, dst, bdst, brd):
            KK.op("dve", lambda v: v.tensor_tensor(out=dst[:], in0=V[:, 0, :],
                                                   in1=V[:, 1, :], op=ALU.add),
                  reads=list(brd), writes=[bdst])
            for cc in range(2, nchunk):
                KK.op("dve", lambda v, cc=cc: v.tensor_tensor(
                    out=dst[:], in0=dst[:], in1=V[:, cc, :], op=ALU.add),
                    reads=list(brd), writes=[bdst])

        def bc_matmul(bknum, rd):
            KK.op("dve", lambda v: v.tensor_copy(out=MACCB[:], in_=MACC[:]),
                  reads=[bMACC], writes=[bMACCB])
            KK.op("pe", lambda t, b=bknum: t.matmul(bank(b), ONEM[:], MACCB[:],
                                                    start=True, stop=True),
                  reads=[bMACCB, bCONST], writes=bk(bknum))

        def rstd_from(bknum, dst, bdst, sub_msq, msq_src=None, bmsq=None):
            if sub_msq:
                KK.op("dve", lambda v, msq_src=msq_src: v.tensor_tensor(
                    out=MACC[:], in0=msq_src[:], in1=msq_src[:], op=ALU.mult),
                    reads=[bmsq], writes=[bMACC])
                KK.op("dve", lambda v, b=bknum: v.tensor_tensor(
                    out=MACC[:], in0=bank(b), in1=MACC[:], op=ALU.subtract),
                    reads=bk(bknum) + [bMACC], writes=[bMACC])
                src, srcb = MACC[:], [bMACC]
            else:
                src, srcb = bank(bknum), bk(bknum)
            KK.op("act", lambda s, src=src: s.activation(out=MACC[:], in_=src,
                                                         func=AF.Ln, bias=EPSC[:]),
                  reads=list(srcb) + [bEPS], writes=[bMACC])
            KK.op("act", lambda s: s.activation(out=dst[:], in_=MACC[:],
                                                func=AF.Exp, scale=-0.5),
                  reads=[bMACC], writes=[bdst])

        NT_eff = 0 if dbg == "comp" else (1 if dbg in ("y1", "yk", "sm", "ao1", "aog", "pipe", "pipe2", "pipeG") else NT)
        no_pv = dbg == "pipe2"
        only_g1 = dbg == "pipeG"
        if dbg in ("pipe2", "pipeG"):
            dbg = "pipe"

        if dbg == "ao1":
            dbg = "ao"

        skip_center = dbg == "yk"
        if dbg in ("y1", "yk"):
            dbg = "y"

        xload = [None] * (NT + 2)
        for i in range(min(2, NT_eff)):
            xload[i] = ld("sp", XBUF[:, i % 2], x_in[:, :, i * T:(i + 1) * T], [bXB[i % 2]])

        for i in range(NT_eff):
            par = i % 2
            XT = XBUF[:, par]
            bX = bXB[par]
            t0 = i * T

            accum_stats(XT, C, MACC, bMACC, [bX])
            bc_matmul(6, [bX])
            KK.op("act", lambda s: s.activation(out=M1S[:], in_=bank(6),
                                                func=AF.Identity),
                  reads=bk(6), writes=[bM1S])
            KK.op("act", lambda s, XT=XT: s.activation(out=SQ[:], in_=XT[:], func=AF.Square),
                  reads=[bX], writes=[bSQ])
            accum_stats(SQ, C, MACC, bMACC, [bSQ])
            bc_matmul(7, [bSQ])
            rstd_from(7, RA, bRA, sub_msq=True, msq_src=M1S, bmsq=bM1S)
            for c4 in range(4):
                KK.op("pe", lambda t, c4=c4: t.matmul(
                    bank(6, c4, c4 + 1), RA[0:1, c4 * P:(c4 + 1) * P], C1F[:],
                    start=True, stop=True),
                    reads=[bRA, bCONST], writes=bk(6))
            KK.op("dve", lambda v: v.tensor_copy(out=R1C[:], in_=bank(6, 0, 4)),
                  reads=bk(6), writes=[bR1C])

            m1b0 = M1S[:].unsqueeze(1).broadcast_to([P, C, T])
            KK.op("dve", lambda v, XT=XT, m1b0=m1b0: (
                v.tensor_tensor(out=XBF[:], in0=XT[:], in1=m1b0, op=ALU.subtract),
                v.drain())[1],
                reads=[bX, bM1S], writes=[bXBF])
            for mc in range(C):
                b = 6 + mc % 2
                for kc in range(C):
                    KK.op("pe", lambda t, mc=mc, kc=kc, b=b: t.matmul(
                        bank(b), QW[:, kc, mc * P:(mc + 1) * P], XBF[:, kc, :],
                        start=(kc == 0), stop=(kc == C - 1)),
                        reads=[bQW, bXBF], writes=bk(b))
                KK.op("act", lambda s, mc=mc, b=b: s.activation(
                    out=QBUF[:, mc, :], in_=bank(b), func=AF.Copy, scale=SCALE),
                    reads=bk(b), writes=[bQBUF])
            for kc in range(C):
                KK.op("pe", lambda t, kc=kc: t.matmul(
                    bank(6), KVW[:, kc, :], XBF[:, kc, :],
                    start=(kc == 0), stop=(kc == C - 1)),
                    reads=[bKVW, bXBF], writes=bk(6))
            KK.op("dve", lambda v: (
                v.tensor_tensor(out=KVS[:], in0=bank(6), in1=RA[:], op=ALU.mult),
                v.drain())[1],
                  reads=bk(6) + [bRA], writes=[bKVS])
            for kc in range(C):
                KK.op("pe", lambda t, kc=kc: t.matmul(
                    bank(7)[64:P, :], KVW[:, kc, 0:HD], XBF[:, kc, :],
                    start=(kc == 0), stop=(kc == C - 1), tile_position=(0, 64)),
                    reads=[bKVW, bXBF], writes=bk(7))
            KK.op("dve", lambda v: (
                v.tensor_tensor(out=K2S[64:P, :], in0=bank(7)[64:P, :],
                                in1=RA[64:P, :], op=ALU.mult),
                v.drain())[1],
                  reads=bk(7) + [bRA], writes=[bK2])

            if dbg == "q":
                out_toks.append(KK.op(
                    "pool", lambda g, t0=t0: g.dma_start(
                        out=out_t[:, :, t0:t0 + T], in_=QBUF[:]),
                    reads=[bQBUF], dma=True))
                if i + 2 < NT:
                    xload[i + 2] = ld("sp", XBUF[:, par],
                                      x_in[:, :, (i + 2) * T:(i + 3) * T], [bXB[par]])
                continue
            if dbg == "kv":
                out_toks.append(KK.op(
                    "pool", lambda g, t0=t0: g.dma_start(
                        out=out_t[:, 0, t0:t0 + T], in_=KVS[:]),
                    reads=[bKVS], dma=True))
                out_toks.append(KK.op(
                    "pool", lambda g, t0=t0: g.dma_start(
                        out=out_t[:, 1, t0:t0 + T], in_=K2S[:]),
                    reads=[bK2], dma=True))
                if i + 2 < NT:
                    xload[i + 2] = ld("sp", XBUF[:, par],
                                      x_in[:, :, (i + 2) * T:(i + 3) * T], [bXB[par]])
                continue

            if dbg == "pipe" and i == 0:
                dump_f32(KK, XBF[:], bXBF, 0, XBUF[:, 1], bXB[1])
                dump_f32(KK, KVS[:].unsqueeze(1), bKVS, 1, XBUF[:, 1], bXB[1], nchunk=1)

            for w in range(NW):
                q0 = w * WIN
                KK.op("pe", lambda t: t.drain(), reads=[], writes=[])
                for u in range(2):
                    KK.op("pe", lambda t, u=u, q0=q0: t.transpose(
                        PSB[:, 6, u * HD:(u + 1) * HD],
                        KVS[64:P, q0 + u * P:q0 + (u + 1) * P],
                        IDENT[64:P, 64:P]),
                        reads=[bKVS, bCONST], writes=bk(6))
                KK.op("pe", lambda t: t.drain(), reads=[], writes=[])
                KK.op("dve", lambda v, w=w: (
                    v.tensor_copy(out=VT[:, w, :, :], in_=PSB[:, 6, 0:2 * HD].rearrange(
                        "p (u d) -> p u d", d=HD)),
                    v.drain())[1],
                    reads=bk(6), writes=[bVT])
                for g in ([1] if (dbg == "aog" or only_g1) else range(4)):
                    if dbg == "pipe" and i == 0 and w == 0 and g == 1:
                        KK.op("dve", lambda v: v.memset(
                            PTS[:].rearrange("p h u s n -> p (h u s n)"), 1.0),
                            reads=[], writes=[bPTS])
                    for half in range(2):
                        pr = (2 * g + half) % 2
                        EB = EB2[:, pr]
                        bEB = bEB2[pr]
                        Sv = sview(pr)
                        qh0 = q0 + half * P
                        for j in range(4):
                            h = 4 * g + j
                            sl = SLOT[j]
                            base = (h % 2) * HD
                            kv_src = KVS if h % 2 == 0 else K2S
                            KK.op("pe", lambda t, h=h, sl=sl, base=base, qh0=qh0,
                                  q0=q0, kv_src=kv_src, Sv=Sv: t.matmul(
                                      Sv[:, sl, :],
                                      QBUF[base:base + HD, h // 2, qh0:qh0 + P],
                                      kv_src[base:base + HD, q0:q0 + WIN],
                                      start=True, stop=True),
                                  reads=[bQBUF, bKVS if h % 2 == 0 else bK2],
                                  writes=bk(2 * pr, 2 * pr + 1))
                        KK.op("pe", lambda t: t.drain(), reads=[], writes=[])
                        KK.op("dve", lambda v, Sv=Sv: v.reduce_max(
                            out=NMX[:], in_=Sv[:], axis=AX.X, negate=True),
                            reads=bk(2 * pr, 2 * pr + 1), writes=[bNMX])
                        KK.op("dve", lambda v, c4=2 * w + half: v.tensor_scalar(
                            out=BS4[:], in0=NMX[:], scalar1=R1C[:, c4:c4 + 1],
                            scalar2=None, op0=ALU.mult),
                            reads=[bNMX, bR1C], writes=[bBS4])
                        for j in range(4):
                            sl = SLOT[j]
                            KK.op("act", lambda s, sl=sl, Sv=Sv, EB=EB, c4=2 * w + half:
                                  s.activation(out=EB[:, sl, :], in_=Sv[:, sl, :],
                                               func=AF.Exp,
                                               bias=BS4[:, sl:sl + 1],
                                               scale=R1C[:, c4:c4 + 1],
                                               accum_out=SUM4[:, sl:sl + 1]),
                                  reads=bk(2 * pr, 2 * pr + 1) + [bBS4, bR1C],
                                  writes=[bEB, bSUM4])
                        KK.op("dve", lambda v: v.reciprocal(out=REC4[:], in_=SUM4[:]),
                              reads=[bSUM4], writes=[bREC4])
                        if dbg == "pipe" and i == 0 and w == 0 and g == 1 and half == 0:
                            out_toks.append(KK.op("pool", lambda g_: g_.dma_start(
                                out=out_t[:, 0, 1536:1540], in_=SUM4[:]),
                                reads=[bSUM4], dma=True))
                            out_toks.append(KK.op("pool", lambda g_: g_.dma_start(
                                out=out_t[:, 0, 1540:1544], in_=REC4[:]),
                                reads=[bREC4], dma=True))
                            out_toks.append(KK.op("pool", lambda g_: g_.dma_start(
                                out=out_t[:, 0, 1544:1548], in_=BS4[:]),
                                reads=[bBS4], dma=True))
                            out_toks.append(KK.op("pool", lambda g_: g_.dma_start(
                                out=out_t[:, 0, 1548:1552], in_=NMX[:]),
                                reads=[bNMX], dma=True))
                            KK.op("dve", lambda v, EB=EB: v.tensor_copy(
                                out=XBUF[:, 1, 4:6, :],
                                in_=EB[:].rearrange("p (c d) n -> p c (d n)", c=2)),
                                reads=[bEB], writes=[bXB[1]])
                            out_toks.append(KK.op("pool", lambda g_: g_.dma_start(
                                out=out_t[:, 4:6, 1536:2048], in_=XBUF[:, 1, 4:6, :]),
                                reads=[bXB[1]], dma=True))
                        for j in range(4):
                            sl = SLOT[j]
                            KK.op("act", lambda s, sl=sl, EB=EB, bEB=bEB: s.activation(
                                out=PB[:, sl, :], in_=EB[:, sl, :],
                                func=AF.Identity, scale=REC4[:, sl:sl + 1]),
                                reads=[bREC4, bEB], writes=[bPB])
                        if dbg == "pipe" and i == 0 and w == 0 and g == 1 and half == 0:
                            KK.op("dve", lambda v: v.tensor_copy(
                                out=XBUF[:, 1, 6:8, :],
                                in_=PB[:].rearrange("p (c d) n -> p c (d n)", c=2)),
                                reads=[bPB], writes=[bXB[1]])
                            out_toks.append(KK.op("pool", lambda g_: g_.dma_start(
                                out=out_t[:, 6:8, 1536:2048], in_=XBUF[:, 1, 6:8, :]),
                                reads=[bXB[1]], dma=True))
                        KK.op("pe", lambda t: t.drain(), reads=[], writes=[])
                        PTv = ptview()
                        for j in (0, 1, 2, 3):
                            sl = SLOT[j]
                            for u in range(2):
                                KK.op("pe", lambda t, sl=sl, u=u, PTv=PTv: t.matmul(
                                    PTv[:, u, sl, :], PB[:, sl, u * P:(u + 1) * P],
                                    IDENT[:], start=True, stop=True),
                                    reads=[bPB, bCONST], writes=bk(6, 7),
                                    tag=f"PTmm_i{i}w{w}g{g}h{half}sl{sl}u{u}")
                        KK.op("dve", lambda v, half=half: v.tensor_copy(
                            out=PTS[:, half].rearrange("p u s n -> p (u s) n"),
                            in_=ptview()[:].rearrange("p u s n -> p (u s) n")),
                            reads=bk(6, 7), writes=[bPTS],
                            tag=f"evac_i{i}w{w}g{g}h{half}")
                        if dbg == "pipe" and i == 0 and w == 0 and g == 1 and half == 0:
                            KK.op("dve", lambda v: v.tensor_copy(
                                out=XBUF[:, 1, 2:4, :],
                                in_=ptview()[:].rearrange("p u s n -> p u (s n)")),
                                reads=bk(6, 7), writes=[bXB[1]])
                            out_toks.append(KK.op("pool", lambda g_: g_.dma_start(
                                out=out_t[:, 2:4, 1536:2048], in_=XBUF[:, 1, 2:4, :]),
                                reads=[bXB[1]], dma=True))
                            KK.op("dve", lambda v: v.tensor_copy(
                                out=XBUF[:, 1, 0:4, :],
                                in_=PTS[:].rearrange("p h u s n -> p (h u) (s n)")),
                                reads=[bPTS], writes=[bXB[1]])
                            out_toks.append(KK.op("pool", lambda g_: g_.dma_start(
                                out=out_t[:, 0:4, 7 * T:8 * T], in_=XBUF[:, 1, 0:4, :]),
                                reads=[bXB[1]], dma=True))
                        if dbg == "sm2" and i == 0 and w == 0 and g == 1 and half == 0:
                            out_toks.append(KK.op("pool", lambda g_: g_.dma_start(
                                out=out_t[:, 0, 7168:8192], in_=EB[:].rearrange("p s n -> p (s n)")),
                                reads=[bEB], dma=True))
                        if dbg == "aog" and i == 0 and w == 0 and g == 1 and half == 1:
                            out_toks.append(KK.op("pool", lambda g_: g_.dma_start(
                                out=out_t[:, 0, 2048:3072], in_=PTS[:, 0].rearrange("p u s n -> p (u s n)")),
                                reads=[bPTS], dma=True))
                            out_toks.append(KK.op("pool", lambda g_: g_.dma_start(
                                out=out_t[:, 0, 3072:4096], in_=PTS[:, 1].rearrange("p u s n -> p (u s n)")),
                                reads=[bPTS], dma=True))
                            KK.op("dve", lambda v, PTv=PTv: v.tensor_copy(
                                out=SQ[:, 0:2, :], in_=PTv[:].rearrange("p u s n -> p (u s) n")),
                                reads=bk(6, 7), writes=[bSQ])
                            out_toks.append(KK.op("pool", lambda g_: g_.dma_start(
                                out=out_t[:, 3, 2048:3072], in_=SQ[:, 0:2, :].rearrange("p c n -> p (c n)")),
                                reads=[bSQ], dma=True))
                            out_toks.append(KK.op("pool", lambda g_: g_.dma_start(
                                out=out_t[:, 1, 2048:2052], in_=REC4[:]),
                                reads=[bREC4], dma=True))
                            out_toks.append(KK.op("pool", lambda g_: g_.dma_start(
                                out=out_t[:, 1, 3200:4224], in_=PB[:].rearrange("p s n -> p (s n)")),
                                reads=[bPB], dma=True))
                            out_toks.append(KK.op("pool", lambda g_: g_.dma_start(
                                out=out_t[:, 1, 3072:3200], in_=VT[:, 0].rearrange("p u d -> p (u d)")),
                                reads=[bVT], dma=True))
                            out_toks.append(KK.op("pool", lambda g_, EB=EB, bEB=bEB: g_.dma_start(
                                out=out_t[:, 2, 2048:3072], in_=EB[:].rearrange("p s n -> p (s n)")),
                                reads=[bEB], dma=True))
                        if dbg == "sm2" and i == 0 and w == 0 and g == 1 and half == 1:
                            out_toks.append(KK.op("pool", lambda g_: g_.dma_start(
                                out=out_t[:, 1, 6144:8192], in_=PTS[:].rearrange("p u s n -> p (u s n)")),
                                reads=[bPTS], dma=True))
                        if dbg == "sm" and i == 0 and w == 0 and g == 1 and half == 0:
                            Svv = sview(pr)
                            out_toks.append(KK.op("dve", lambda v, Svv=Svv: v.tensor_copy(
                                out=SQ[:, 0:2, :].rearrange("p c n -> p (c n)"), in_=Svv[:].rearrange("p s n -> p (s n)")),
                                reads=bk(2 * pr, 2 * pr + 1), writes=[bSQ]))
                            out_toks.append(KK.op("pool", lambda g_: g_.dma_start(
                                out=out_t[:, 0, 0:1024], in_=SQ[:, 0:2, :].rearrange("p c n -> p (c n)")),
                                reads=[bSQ], dma=True))
                            out_toks.append(KK.op("pool", lambda g_: g_.dma_start(
                                out=out_t[:, 2, 0:1024], in_=EB[:].rearrange("p s n -> p (s n)")),
                                reads=[bEB], dma=True))
                            out_toks.append(KK.op("pool", lambda g_: g_.dma_start(
                                out=out_t[:, 3, 0:4], in_=NMX[:]), reads=[bNMX], dma=True))
                            out_toks.append(KK.op("pool", lambda g_: g_.dma_start(
                                out=out_t[:, 3, 4:8], in_=BS4[:]), reads=[bBS4], dma=True))
                            out_toks.append(KK.op("pool", lambda g_: g_.dma_start(
                                out=out_t[:, 3, 8:12], in_=SUM4[:]), reads=[bSUM4], dma=True))
                            out_toks.append(KK.op("pool", lambda g_: g_.dma_start(
                                out=out_t[:, 3, 12:16], in_=REC4[:]), reads=[bREC4], dma=True))
                            out_toks.append(KK.op("pool", lambda g_: g_.dma_start(
                                out=out_t[:, 3, 16:20], in_=R1C[:]), reads=[bR1C], dma=True))
                    if dbg == "pipe" and i == 0 and w == 0 and g == 1:
                        KK.op("dve", lambda v: v.tensor_copy(
                            out=XBUF[:, 1, 0:4, :],
                            in_=PTS[:].rearrange("p h u s n -> p (h u) (s n)")),
                            reads=[bPTS], writes=[bXB[1]])
                        out_toks.append(KK.op(
                            "pool", lambda g_: g_.dma_start(
                                out=out_t[:, 0:4, 5 * T:6 * T],
                                in_=XBUF[:, 1, 0:4, :]),
                            reads=[bXB[1]], dma=True))
                    if no_pv:
                        continue
                    KK.op("pe", lambda t: t.drain(), reads=[], writes=[])
                    for ci, cch in enumerate((2 * g, 2 * g + 1)):
                        for h in (2 * cch, 2 * cch + 1):
                            sl = SLOT[h - 4 * g]
                            odd = h % 2
                            pb_ = 4 + odd
                            r0 = 64 * odd
                            n0 = WIN * ci
                            for u in range(2):
                                KK.op("pe", lambda t, w=w, u=u, sl=sl, odd=odd, r0=r0,
                                      n0=n0, pb_=pb_: t.matmul(
                                          bank(pb_, n0, n0 + WIN)[r0:r0 + HD, :],
                                          VT[:, w, u, :], PTS[:, :, u, sl, :],
                                          start=(u == 0), stop=(u == 1),
                                          tile_position=(0, 64 * odd)),
                                      reads=[bVT, bPTS], writes=bk(pb_))
                    for ci, cch in enumerate((2 * g, 2 * g + 1)):
                        n0 = WIN * ci
                        KK.op("act", lambda s, cch=cch, n0=n0: s.activation(
                            out=AOB[0:HD, cch, :], in_=bank(4, n0, n0 + WIN)[0:HD, :],
                            func=AF.Identity),
                            reads=bk(4), writes=[bAOB])
                        KK.op("act", lambda s, cch=cch, n0=n0: s.activation(
                            out=AOB[HD:P, cch, :], in_=bank(5, n0, n0 + WIN)[HD:P, :],
                            func=AF.Identity),
                            reads=bk(5), writes=[bAOB])
                    if dbg == "sm2" and i == 0 and w == 0 and g == 1:
                        out_toks.append(KK.op("pool", lambda g_: g_.dma_start(
                            out=out_t[:, 2, 7680:8192], in_=AOB[:, 2:4, :].rearrange("p c n -> p (c n)")),
                            reads=[bAOB], dma=True))
                if dbg == "pipe" and i == 0 and w == 0:
                    KK.op("dve", lambda v: v.tensor_copy(
                        out=XBUF[:, 1, :, 0:WIN], in_=AOB[:]),
                        reads=[bAOB], writes=[bXB[1]])
                    out_toks.append(KK.op(
                        "pool", lambda g_: g_.dma_start(
                            out=out_t[:, :, 2 * T:2 * T + WIN],
                            in_=XBUF[:, 1, :, 0:WIN]),
                        reads=[bXB[1]], dma=True))
                if dbg in ("ao", "aog"):
                    out_toks.append(KK.op(
                        "pool", lambda g, t0=t0, w=w: g.dma_start(
                            out=out_t[:, :, t0 + w * WIN:t0 + (w + 1) * WIN],
                            in_=AOB[:]),
                        reads=[bAOB], dma=True))
                    continue
                if dbg in ("sm", "sm2"):
                    continue

                KK.op("pe", lambda t: t.drain(), reads=[], writes=[])
                for mc in range(C):
                    b = 6 + mc % 2
                    for kc in range(C):
                        KK.op("pe", lambda t, mc=mc, kc=kc, b=b: t.matmul(
                            bank(b, 0, WIN), PW[:, kc, mc * P:(mc + 1) * P],
                            AOB[:, kc, :], start=(kc == 0),
                            stop=(kc == C - 1)),
                            reads=[bPW, bAOB], writes=bk(b))
                    KK.op("act", lambda s, mc=mc, w=w, b=b: s.activation(
                        out=YB[:, mc, w * WIN:(w + 1) * WIN], in_=bank(b, 0, WIN),
                        func=AF.Identity, bias=PBF[:, mc:mc + 1]),
                        reads=bk(b) + [bPBF], writes=[bYB])

            if dbg in ("ao", "aog", "sm", "sm2"):
                if i + 2 < NT:
                    xload[i + 2] = ld("sp", XBUF[:, par],
                                      x_in[:, :, (i + 2) * T:(i + 3) * T], [bXB[par]])
                continue
            if dbg == "y":
                out_toks.append(KK.op(
                    "pool", lambda g, t0=t0: g.dma_start(
                        out=out_t[:, :, t0:t0 + T], in_=YB[:]),
                    reads=[bYB], dma=True))
                if i + 2 < NT:
                    xload[i + 2] = ld("sp", XBUF[:, par],
                                      x_in[:, :, (i + 2) * T:(i + 3) * T], [bXB[par]])
                continue

            if dbg == "pipe" and i == 0:
                dump_f32(KK, YB[:], bYB, 4, XBUF[:, 1], bXB[1])

            accum_stats(YB, C, MACC, bMACC, [bYB])
            bc_matmul(6, [bYB])
            KK.op("act", lambda s: s.activation(out=M2S[:], in_=bank(6),
                                                func=AF.Copy, scale=-1.0),
                  reads=bk(6), writes=[bM2S])
            KK.op("act", lambda s: s.activation(out=SQ[:], in_=YB[:], func=AF.Square),
                  reads=[bYB], writes=[bSQ])
            accum_stats(SQ, C, MACC, bMACC, [bSQ])
            bc_matmul(7, [bSQ])
            rstd_from(7, RB, bRB, sub_msq=True, msq_src=M2S, bmsq=bM2S)
            rbb = RB[:].unsqueeze(1).broadcast_to([P, C, T])
            KK.op("dve", lambda v, rbb=rbb: v.tensor_tensor(
                out=SQ[:], in0=YB[:], in1=rbb, op=ALU.mult),
                reads=[bYB, bRB], writes=[bSQ])
            KK.op("dve", lambda v, XT=XT: v.tensor_tensor(out=XT[:], in0=XT[:], in1=SQ[:],
                                                   op=ALU.add),
                  reads=[bSQ, bX], writes=[bX])
            KK.op("dve", lambda v: v.tensor_tensor(out=MACC[:], in0=M2S[:],
                                                   in1=RB[:], op=ALU.mult),
                  reads=[bM2S, bRB], writes=[bMACC])
            mrb2 = MACC[:].unsqueeze(1).broadcast_to([P, C, T])
            KK.op("dve", lambda v, mrb2=mrb2, XT=XT: v.tensor_tensor(
                out=XT[:], in0=XT[:], in1=mrb2, op=ALU.add),
                reads=[bMACC, bX], writes=[bX])

            if dbg == "pipe":
                dump_f32(KK, XT[:], bX, 6, XBUF[:, 1], bXB[1])
                continue
            if dbg == "t1":
                out_toks.append(KK.op(
                    "pool", lambda g, t0=t0, XT=XT: g.dma_start(
                        out=out_t[:, :, t0:t0 + T], in_=XT[:]),
                    reads=[bX], dma=True))
                if i + 2 < NT:
                    xload[i + 2] = ld("sp", XBUF[:, par],
                                      x_in[:, :, (i + 2) * T:(i + 3) * T], [bXB[par]])
                continue

            KK.op("act", lambda s, XT=XT: s.activation(out=SQ[:], in_=XT[:], func=AF.Square),
                  reads=[bX], writes=[bSQ])
            accum_stats(SQ, C, MACC, bMACC, [bSQ])
            bc_matmul(7, [bSQ])
            rstd_from(7, RA, bRA, sub_msq=True, msq_src=M1S, bmsq=bM1S)
            m1b = M1S[:].unsqueeze(1).broadcast_to([P, C, T])
            KK.op("dve", lambda v, m1b=m1b, XT=XT: v.tensor_tensor(
                out=SQ[:], in0=XT[:], in1=m1b, op=ALU.subtract),
                reads=[bX, bM1S], writes=[bSQ])
            rab = RA[:].unsqueeze(1).broadcast_to([P, C, T])
            KK.op("dve", lambda v, rab=rab: (
                v.tensor_tensor(out=LNBF[:], in0=SQ[:], in1=rab, op=ALU.mult),
                v.drain())[1],
                reads=[bSQ, bRA], writes=[bLNBF])

            for ms in range(16):
                sl = ms % 2
                ld("sp", WEXPS[:, sl],
                   wexpP[ms].rearrange("kc p m -> p kc m"), [bWES[sl]],
                   reads=[bWEXP_H])
                for m2 in range(2):
                    hcc = 2 * ms + m2
                    b = 6 + m2
                    for kc in range(C):
                        KK.op("pe", lambda t, sl=sl, m2=m2, kc=kc, b=b: t.matmul(
                            bank(b), WEXPS[:, sl, kc, m2 * P:(m2 + 1) * P],
                            LNBF[:, kc, :], start=(kc == 0), stop=(kc == C - 1)),
                            reads=[bWES[sl], bLNBF], writes=bk(b))
                    KK.op("act", lambda s, hcc=hcc, b=b: s.activation(
                        out=GB[:, hcc, :], in_=bank(b), func=AF.Gelu,
                        bias=B1COL[:, hcc:hcc + 1]),
                        reads=bk(b) + [bB1], writes=[bGB[hcc]])
            for mc in range(C):
                sl = mc % 2
                ld("sp", WCONS[:, sl], wconP[mc].rearrange("kc p m -> p kc m"),
                   [bWCS[sl]], reads=[bWCON_H])
                b = 6 + mc % 2
                for kc in range(HC):
                    KK.op("pe", lambda t, sl=sl, kc=kc, b=b: t.matmul(
                        bank(b), WCONS[:, sl, kc, :], GB[:, kc, :],
                        start=(kc == 0), stop=(kc == HC - 1)),
                        reads=[bWCS[sl], bGB[kc]], writes=bk(b))
                KK.op("act", lambda s, mc=mc, b=b: s.activation(
                    out=YB[:, mc, :], in_=bank(b), func=AF.Identity,
                    bias=B2COL[:, mc:mc + 1]),
                    reads=bk(b) + [bB2], writes=[bYB])

            accum_stats(YB, C, MACC, bMACC, [bYB])
            bc_matmul(6, [bYB])
            KK.op("act", lambda s: s.activation(out=M1S[:], in_=bank(6),
                                                func=AF.Copy, scale=-1.0),
                  reads=bk(6), writes=[bM1S])
            KK.op("act", lambda s: s.activation(out=SQ[:], in_=YB[:], func=AF.Square),
                  reads=[bYB], writes=[bSQ])
            accum_stats(SQ, C, MACC, bMACC, [bSQ])
            bc_matmul(7, [bSQ])
            rstd_from(7, RB, bRB, sub_msq=True, msq_src=M1S, bmsq=bM1S)
            rbb2 = RB[:].unsqueeze(1).broadcast_to([P, C, T])
            KK.op("dve", lambda v, rbb2=rbb2: v.tensor_tensor(
                out=SQ[:], in0=YB[:], in1=rbb2, op=ALU.mult),
                reads=[bYB, bRB], writes=[bSQ])
            KK.op("dve", lambda v, XT=XT: v.tensor_tensor(out=XT[:], in0=XT[:], in1=SQ[:],
                                                   op=ALU.add),
                  reads=[bSQ, bX], writes=[bX])
            KK.op("dve", lambda v: v.tensor_tensor(out=MACC[:], in0=M1S[:],
                                                   in1=RB[:], op=ALU.mult),
                  reads=[bM1S, bRB], writes=[bMACC])
            mrb = MACC[:].unsqueeze(1).broadcast_to([P, C, T])
            KK.op("dve", lambda v, mrb=mrb, XT=XT: v.tensor_tensor(
                out=XT[:], in0=XT[:], in1=mrb, op=ALU.add),
                reads=[bMACC, bX], writes=[bX])

            if dbg not in ("sm", "sm2"):
                out_toks.append(KK.op(
                    "pool", lambda g, t0=t0, XT=XT: g.dma_start(
                        out=out_t[:, :, t0:t0 + T], in_=XT[:]),
                    reads=[bX], dma=True))
            if i + 2 < NT:
                xload[i + 2] = ld("sp", XBUF[:, par],
                                  x_in[:, :, (i + 2) * T:(i + 3) * T], [bXB[par]])

        KK.op("sp", lambda s: s.nop(), deps=list(out_toks) + list(dbg_toks))
        S.flush(nc, block)

    return nc


_CACHE = {}


def kernel(L0_attn_x, L0_attn_q_w, L0_attn_kv_w, L0_attn_proj_w, L0_attn_proj_b,
           L0_ffn_W_exp, L0_ffn_b_exp, L0_ffn_up_weights, L0_ffn_up_bias,
           L0_ffn_down_weights, L0_ffn_W_con, L0_ffn_b_con, L0_ffn_down_bias):
    x = np.asarray(L0_attn_x, np.float32)
    qw = np.asarray(L0_attn_q_w, np.float32)
    kvw = np.asarray(L0_attn_kv_w, np.float32)
    pw = np.asarray(L0_attn_proj_w, np.float32)
    pb = np.asarray(L0_attn_proj_b, np.float32)
    wexp = np.asarray(L0_ffn_W_exp, np.float32)
    bexp = np.asarray(L0_ffn_b_exp, np.float32)
    upw = np.asarray(L0_ffn_up_weights, np.float32)
    upb = np.asarray(L0_ffn_up_bias, np.float32)
    dnw = np.asarray(L0_ffn_down_weights, np.float32)
    wcon = np.asarray(L0_ffn_W_con, np.float32)
    bcon = np.asarray(L0_ffn_b_con, np.float32)
    dwnb = np.asarray(L0_ffn_down_bias, np.float32)

    qwT = chunked(np.ascontiguousarray(qw.T))
    kvwT = chunked(np.ascontiguousarray(kvw.T))
    pwT = chunked(np.ascontiguousarray(pw.T))
    pbc = chunked(pb)
    wexpT = chunked(np.ascontiguousarray(wexp.T))
    wconN = chunked(wcon)
    bexpc = np.zeros((DH, 2), np.float32)
    bexpc[:, 0] = bexp
    bexpc = chunked(bexpc)
    upbc = chunked(upb)
    bconc = chunked(bcon)
    dwnbc = chunked(dwnb)
    icU = stage_lhsT_inchunk(upw, False)
    icD = stage_lhsT_inchunk(dnw, True)
    dgU = stage_diag_blocks(upw, False)
    dgD = stage_diag_blocks(dnw, True)
    ident = np.eye(P, dtype=BFDT)
    onem = np.full((P, P), 1.0 / D, BFDT)
    onecol = np.ones((P, 1), BFDT)
    onerow = np.ones((1, P), BFDT)
    c1f = np.ones((1, 1), np.float32)

    shared = {
        "qw_in": qwT, "kvw_in": kvwT, "pw_in": pwT, "pb_in": pbc,
        "wexpT_in": wexpT, "wconN_in": wconN, "bexp_in": bexpc,
        "upb_in": upbc, "bcon_in": bconc, "dwnb_in": dwnbc,
        "icU_in": icU, "icD_in": icD, "dgU_in": dgU, "dgD_in": dgD,
        "ident_in": ident, "onem_in": onem, "onecol_in": onecol,
        "onerow_in": onerow, "c1f_in": c1f,
    }

    in_maps = []
    for ci in range(NCORES):
        xc = x[2 * ci:2 * ci + 2].reshape(TOK, D).T
        xc = np.ascontiguousarray(
            xc.reshape(C, P, TOK).transpose(1, 0, 2))
        m = dict(shared)
        m["x_in"] = xc
        in_maps.append(m)

    if "nc" not in _CACHE:
        _CACHE["nc"] = build_program()
    res = run_bass_kernel_spmd(_CACHE["nc"], in_maps, core_ids=list(range(NCORES)))

    out = np.empty((BATCH, SEQ, D), np.float32)
    for ci in range(NCORES):
        oc = res.results[ci]["out_t"]
        oc = oc.transpose(1, 0, 2).reshape(D, TOK).T
        out[2 * ci:2 * ci + 2] = oc.reshape(2, SEQ, D)
    return out


if __name__ == "__main__":
    import json
    np.random.seed(0)
    print("building program...")
    nc = build_program()
    print("n_inst:", sum(1 for _ in nc.all_instructions()))
